# version 02 (final)
# baseline (speedup 1.0000x reference)
import contextlib
import numpy as np
import ml_dtypes
import concourse.bass as bass
import concourse.mybir as mybir
from concourse.alu_op_type import AluOpType as OP
from concourse.bass_utils import run_bass_kernel_spmd

F32 = mybir.dt.float32
BF16 = mybir.dt.bfloat16
I32 = mybir.dt.int32
AF = mybir.ActivationFunctionType
AX = mybir.AxisListType.X
BF = ml_dtypes.bfloat16

D = 2048
H = 16
DK = 128
DV = 128
M = 64
HM = H * M
FD = 1408
E = 8
SCALE = DK ** -0.5
EPS = 1e-5
LNEPS = 1e-5


class Buf:
    __slots__ = ("w", "readers")

    def __init__(self):
        self.w = None
        self.readers = {}


class Prog:
    def __init__(self):
        self.ops = {e: [] for e in ("t", "v", "s", "sy", "g")}
        self.cnt = {e: 0 for e in self.ops}
        self.waited = {e: {} for e in self.ops}

    def emit(self, eng, fns, reads=(), writes=()):
        want = {}
        for b in reads:
            if b.w is not None:
                e2, c2 = b.w
                if want.get(e2, 0) < c2:
                    want[e2] = c2
        for b in writes:
            if b.w is not None:
                e2, c2 = b.w
                if want.get(e2, 0) < c2:
                    want[e2] = c2
            for e2, c2 in b.readers.items():
                if want.get(e2, 0) < c2:
                    want[e2] = c2
        for e2, c2 in sorted(want.items()):
            if self.waited[eng].get(e2, 0) < c2:
                self.ops[eng].append(("w", e2, c2))
                self.waited[eng][e2] = c2
        self.cnt[eng] += 1
        c = self.cnt[eng]
        if not isinstance(fns, (list, tuple)):
            fns = [fns]
        self.ops[eng].append(("o", fns, c))
        for b in reads:
            b.readers[eng] = c
        for b in writes:
            b.w = (eng, c)
            b.readers = {}

    def barrier(self):
        for eng in self.ops:
            for e2 in self.ops:
                if e2 == eng:
                    continue
                c2 = self.cnt[e2]
                if c2 and self.waited[eng].get(e2, 0) < c2:
                    self.ops[eng].append(("w", e2, c2))
                    self.waited[eng][e2] = c2


def build_nc(T):
    NCH = T // 128
    OCH = NCH // 4
    OT = 128 * OCH
    WSZ = min(512, OT)
    NW = OT // WSZ
    NT4 = WSZ // 128
    nc = bass.Bass("TRN2", target_bir_lowering=False)
    pr = Prog()
    B_ = Buf

    dt = lambda n, shp, dty, kind: nc.dram_tensor(n, shp, dty, kind=kind).ap()
    EI, IN = "ExternalInput", "Internal"

    x_full = dt("x_full", [T, D], F32, EI)
    x_own = dt("x_own", [OT, D], F32, EI)
    wqT = dt("wqT", [D, D], F32, EI)
    wkT = dt("wkT", [D, D], F32, EI)
    wvT = dt("wvT", [D, D], F32, EI)
    wgT = dt("wgT", [D, D], F32, EI)
    woT = dt("woT", [D, D], F32, EI)
    ws1 = dt("ws1", [16, D], F32, EI)
    ws2T = dt("ws2T", [16, HM], F32, EI)
    bs2 = dt("bs2", [1, HM], F32, EI)
    gnw = dt("gnw", [1, DV], F32, EI)
    gwT = dt("gwT", [D, E], F32, EI)
    gpT = dt("gpT", [E, D, FD], F32, EI)
    upT = dt("upT", [E, D, FD], F32, EI)
    dpT = dt("dpT", [E, FD, D], F32, EI)
    cosK = dt("cosK", [128, NCH * 64], BF16, EI)
    sinK = dt("sinK", [128, NCH * 64], BF16, EI)
    cosQ = dt("cosQ", [128, OCH * 64], BF16, EI)
    sinQ = dt("sinQ", [128, OCH * 64], BF16, EI)
    maskU = dt("maskU", [128, 128], BF16, EI)
    identD = dt("identD", [128, 128], BF16, EI)
    ones1D = dt("ones1D", [1, 128], BF16, EI)
    selED = dt("selED", [8, E * 128], BF16, EI)
    idx_tok = dt("idx_tok", [128, OCH], I32, EI)
    idx_s1 = dt("idx_s1", [128, 1], I32, EI)
    idx_s2 = dt("idx_s2", [64, 1], I32, EI)
    idx_cr = dt("idx_cr", [2, 1], I32, EI)
    out = dt("out", [OT, D], F32, "ExternalOutput")

    esA = dt("esA", [T, HM], BF16, IN)
    xlnA = dt("xlnA", [T, D], BF16, IN)
    xltA = dt("xltA", [16, 128, T], BF16, IN)
    carryT = dt("carryT", [NCH, HM], F32, IN)
    krA = dt("krA", [T, D], BF16, IN)
    vA = dt("vA", [T, D], BF16, IN)
    snap1 = dt("snap1", [4, 128, HM], BF16, IN)
    snap2 = dt("snap2", [4, 64, H * DV], BF16, IN)
    qrS = dt("qrS", [OT, D], BF16, IN)
    gsS = dt("gsS", [OT, D], F32, IN)
    ogS = dt("ogS", [H, 128, OT], BF16, IN)
    x2S = dt("x2S", [OT, D], F32, IN)
    xl2T = dt("xl2T", [16, 128, OT], BF16, IN)
    accS = dt("accS", [OT, D], F32, IN)
    gpB = dt("gpB", [E, 16, 128, FD], BF16, IN)
    upB = dt("upB", [E, 16, 128, FD], BF16, IN)
    dpB = dt("dpB", [E, 11, 128, D], BF16, IN)

    st = contextlib.ExitStack()
    sb = lambda n, shp, dty: st.enter_context(nc.sbuf_tensor(n, shp, dty))
    ps = lambda n, shp, dty: st.enter_context(nc.psum_tensor(n, shp, dty))

    cosq = sb("cosq", [128, OCH * 64], BF16)
    sinq = sb("sinq", [128, OCH * 64], BF16)
    ckb = [sb("ckb0", [128, 64], BF16), sb("ckb1", [128, 64], BF16)]
    skb = [sb("skb0", [128, 64], BF16), sb("skb1", [128, 64], BF16)]
    msk = sb("msk", [128, 128], BF16)
    idn = sb("idn", [128, 128], BF16)
    on1 = sb("on1", [1, 128], BF16)
    onc = sb("onc", [128, 1], BF16)
    selE = sb("selE", [8, E * 128], BF16)
    gnwb = sb("gnwb", [128, DV], F32)
    gnw1b = sb("gnw1b", [1, DV], BF16)
    bs2b = sb("bs2b", [1, HM], BF16)
    ws1b = sb("ws1b", [16, D], BF16)
    ws2b = sb("ws2b", [16, HM], BF16)
    gwb = sb("gwb", [128, 128], BF16)
    itok = sb("itok", [128, OCH], I32)
    is1 = sb("is1", [128, 1], I32)
    is2 = sb("is2", [64, 1], I32)
    icr = sb("icr", [2, 1], I32)
    crg = sb("crg", [2, HM], F32)
    carry = sb("carry", [1, HM], F32)
    carryb = sb("carryb", [1, HM], BF16)
    s1f = sb("s1f", [128, HM], F32)
    s1b = sb("s1b", [128, HM], BF16)
    s2f = sb("s2f", [64, H * DV], F32)
    s2b = sb("s2b", [64, H * DV], BF16)
    lnst2p = [sb(f"lnst{i}", [128, 8], F32) for i in range(2)]
    b_ln2p = [B_(), B_()]
    lnst = lnst2p[0]
    epsb = sb("epsb", [128, 2], F32)
    sm = sb("sm", [128, 8], F32)
    twTb = sb("twTb", [8, OT], BF16)
    lg = sb("lg", [128, 8], F32)
    lg2 = sb("lg2", [128, 8], F32)
    lgm = sb("lgm", [128, 8], F32)
    lgm2 = sb("lgm2", [128, 8], F32)
    twf = sb("twf", [128, 8], F32)
    twb = sb("twb", [128, 8], BF16)
    rt_sets = [[sb(f"rt{s}_{i}", [128, 64], F32) for i in range(4)] for s in range(4)]
    hsm = [sb(f"hsm{i}", [128, 128], BF16) for i in range(6)]
    qvs = sb("qvs", [128, 64], F32)
    qve = sb("qve", [128, 64], F32)
    qvp = sb("qvp", [128, 64], BF16)
    sqb = sb("sqb", [128, 128], F32)
    onb = sb("onb", [128, 128], F32)
    ogb = sb("ogb", [128, 128], BF16)

    AW = sb("AW", [128, 36864], BF16)
    AB = sb("AB", [128, 16384], BF16)
    AFr = sb("AFr", [128, 10240], F32)

    pA = ps("pA", [128, 512], F32)
    pB = ps("pB", [128, 512], F32)
    pC = ps("pC", [128, 1024], F32)
    pH1 = ps("pH1", [128, 512], F32)
    pH2 = ps("pH2", [128, 512], F32)
    pT1 = ps("pT1", [128, 512], BF16)
    pT2 = ps("pT2", [128, 512], BF16)

    b_consts, b_states, b_carry, b_ln, b_sm = (B_() for _ in range(5))
    b_pA, b_pB, b_pC, b_pH1, b_pH2, b_pT1, b_pT2 = (B_() for _ in range(7))
    (b_esA, b_xlnA, b_xltA, b_carryT, b_krA, b_vA, b_sn1, b_sn2, b_qrS, b_gsS,
     b_ogS, b_x2S, b_xl2T, b_accS, b_gpB, b_upB, b_dpB, b_out, b_hsm,
     b_router, b_twT, b_ck) = (B_() for _ in range(22))
    b_rt_sets = [B_() for _ in range(4)]

    V, S, Tn, SY, G = "v", "s", "t", "sy", "g"

    def dma(q, o, i, rd, wr):
        eng = nc.sync if q == SY else nc.gpsimd
        pr.emit(q, lambda o=o, i=i, eng=eng: eng.dma_start(out=o, in_=i), rd, wr)

    def cast(eng, o, i, rd, wr):
        if eng == S:
            pr.emit(S, lambda o=o, i=i: nc.scalar.copy(out=o, in_=i), rd, wr)
        else:
            pr.emit(V, lambda o=o, i=i: nc.vector.tensor_copy(out=o, in_=i), rd, wr)

    def ts1(out_, in0, s1_, op0, rd, wr):
        pr.emit(V, lambda: nc.vector.tensor_scalar(out=out_, in0=in0, scalar1=s1_,
                scalar2=0.0, op0=op0, op1=OP.add), rd, wr)

    ws1f = AFr[0:16, 0:2048]
    ws2f = AFr[0:16, 2048:3072]
    bs2f = AFr[0:1, 3072:4096]
    gwf = AFr[:, 4096:4224]
    gnw1 = AFr[0:1, 4224:4352]
    b_stage = B_()
    for o, i in ((cosq[:], cosQ), (sinq[:], sinQ),
                 (msk[:], maskU), (idn[:], identD), (on1[:], ones1D), (selE[:], selED),
                 (itok[:], idx_tok), (is1[:], idx_s1), (is2[:], idx_s2), (icr[:], idx_cr),
                 (gnw1, gnw), (bs2f, bs2), (ws1f, ws1), (ws2f, ws2T),
                 (gwf, gwT.rearrange("(a p) e -> p a e", p=128))):
        dma(SY, o, i, [b_consts], [b_consts])
    dma(SY, onc[:, 0:1], ones1D.rearrange("a b -> b a"), [b_consts], [b_consts])
    cast(S, bs2b[:], bs2f, [b_consts], [b_consts])
    cast(S, ws1b[:], ws1f, [b_consts], [b_consts])
    cast(S, ws2b[:], ws2f, [b_consts], [b_consts])
    cast(S, gwb[:], gwf, [b_consts], [b_consts])
    cast(S, gnw1b[:], gnw1, [b_consts], [b_consts])
    pr.emit(V, lambda: nc.vector.memset(epsb[:, 0:1], LNEPS), [b_consts], [b_consts])
    pr.emit(V, lambda: nc.vector.memset(epsb[:, 1:2], EPS), [b_consts], [b_consts])
    pr.emit(Tn, lambda: nc.tensor.matmul(pC[:, 0:DV], on1[0:1, :], gnw1b[0:1, :], start=True, stop=True),
            [b_consts], [b_pC])
    pr.emit(V, lambda: nc.vector.tensor_copy(out=gnwb[:], in_=pC[:, 0:DV]), [b_pC], [b_consts])

    cb = [AFr[:, 4352:6400], AFr[:, 6400:8448]]
    co = [AB[:, 0:2048], AB[:, 2048:4096]]
    b_cb = [B_(), B_()]
    b_co = [B_(), B_()]
    for (src, dst, bdst, nj, w) in ((gpT, gpB, b_gpB, 16, FD), (upT, upB, b_upB, 16, FD),
                                    (dpT, dpB, b_dpB, 11, D)):
        for e in range(E):
            for j in range(nj):
                p = j % 2
                dma(G, cb[p][:, 0:w], src[e, j * 128:(j + 1) * 128, :], [b_cb[p]], [b_cb[p]])
                cast(S if p == 0 else V, co[p][:, 0:w], cb[p][:, 0:w], [b_cb[p]], [b_co[p]])
                dma(G, dst[e, j, :, :], co[p][:, 0:w], [b_co[p]], [bdst, b_co[p]])
    pr.barrier()

    wsb = AW[:, 0:16 * HM]
    b_wsb = B_()
    for j in range(16):
        for half in range(2):
            pp, bp = (pA, b_pA) if half == 0 else (pB, b_pB)
            pr.emit(Tn, lambda j=j, half=half, pp=pp: nc.tensor.matmul(
                pp[:, 0:512], ws1b[:, j * 128:(j + 1) * 128], ws2b[:, half * 512:(half + 1) * 512],
                start=True, stop=True), [b_consts], [bp])
            pr.emit(V, lambda j=j, half=half, pp=pp: nc.vector.tensor_copy(
                out=wsb[:, j * HM + half * 512: j * HM + (half + 1) * 512], in_=pp[:, 0:512]),
                [bp], [b_wsb])
    pr.emit(V, lambda: nc.vector.memset(carry[:], 0.0), [b_carry], [b_carry])

    xch = [AFr[:, 0:2048], AFr[:, 2048:4096]]
    xsq = [AFr[:, 4096:6144], AFr[:, 6144:8192]]
    scl = [AFr[:, 8192:9216], AFr[:, 9216:10240]]
    xln = [AB[:, 0:2048], AB[:, 2048:4096]]
    xlt = [AB[:, 4096:6144], AB[:, 6144:8192]]
    esb = [AB[:, 8192:9216], AB[:, 9216:10240]]
    b_x = [B_(), B_()]
    b_xsq = [B_(), B_()]
    b_scl = [B_(), B_()]
    b_xln = [B_(), B_()]
    b_xlt = [B_(), B_()]
    b_es = [B_(), B_()]

    def ln_pipe(xbuf, b_xb, sqbuf, b_sq, xout, b_xout, lnst=None, b_ln=None):
        pr.emit(V, lambda: nc.vector.reduce_sum(lnst[:, 0:1], xbuf, axis=AX), [b_xb, b_ln], [b_ln])
        pr.emit(S, lambda: nc.scalar.activation(out=sqbuf, in_=xbuf, func=AF.Square), [b_xb, b_sq], [b_sq])
        pr.emit(V, lambda: nc.vector.reduce_sum(lnst[:, 1:2], sqbuf, axis=AX), [b_sq, b_ln], [b_ln])
        pr.emit(S, lambda: nc.scalar.mul(out=lnst[:, 2:3], in_=lnst[:, 0:1], mul=1.0 / D), [b_ln], [b_ln])
        pr.emit(V, lambda: nc.vector.tensor_tensor(out=lnst[:, 3:4], in0=lnst[:, 2:3], in1=lnst[:, 2:3], op=OP.mult), [b_ln], [b_ln])
        pr.emit(S, lambda: nc.scalar.mul(out=lnst[:, 4:5], in_=lnst[:, 1:2], mul=1.0 / D), [b_ln], [b_ln])
        pr.emit(V, lambda: nc.vector.tensor_tensor(out=lnst[:, 4:5], in0=lnst[:, 4:5], in1=lnst[:, 3:4], op=OP.subtract), [b_ln], [b_ln])
        pr.emit(S, lambda: nc.scalar.activation(out=lnst[:, 5:6], in_=lnst[:, 4:5], func=AF.Sqrt, bias=epsb[:, 0:1]), [b_ln, b_consts], [b_ln])
        pr.emit(V, lambda: nc.vector.reciprocal(lnst[:, 5:6], lnst[:, 5:6]), [b_ln], [b_ln])
        pr.emit(S, lambda: nc.scalar.mul(out=lnst[:, 6:7], in_=lnst[:, 2:3], mul=-1.0), [b_ln], [b_ln])
        pr.emit(V, lambda: nc.vector.tensor_scalar(out=xout, in0=xbuf, scalar1=lnst[:, 6:7],
                scalar2=lnst[:, 5:6], op0=OP.add, op1=OP.mult), [b_xb, b_ln, b_xout], [b_xout])

    for c in range(NCH):
        p = c % 2
        dma(SY, xch[p], x_full[c * 128:(c + 1) * 128, :], [b_x[p]], [b_x[p]])
        ln_pipe(xch[p], b_x[p], xsq[p], b_xsq[p], xln[p], b_xln[p], lnst2p[p], b_ln2p[p])
        dma(SY, xlnA[c * 128:(c + 1) * 128, :], xln[p], [b_xln[p]], [b_xlnA])
        for j in range(16):
            pt, bt = (pT1, b_pT1) if j % 2 == 0 else (pT2, b_pT2)
            pr.emit(Tn, lambda p=p, j=j, pt=pt: nc.tensor.transpose(
                pt[:, 0:128], xln[p][:, j * 128:(j + 1) * 128], idn[:]), [b_xln[p], b_consts], [bt])
            pr.emit(V, lambda p=p, j=j, pt=pt: nc.vector.tensor_copy(
                out=xlt[p][:, j * 128:(j + 1) * 128], in_=pt[:, 0:128]), [bt], [b_xlt[p]])
        dma(SY, xltA[:, :, c * 128:(c + 1) * 128].rearrange("a p b -> p a b"), xlt[p], [b_xlt[p]], [b_xltA])
        for half in range(2):
            pp, bp = (pA, b_pA) if half == 0 else (pB, b_pB)
            pr.emit(Tn, [(lambda p=p, j=j2, half=half, pp=pp: nc.tensor.matmul(
                pp[:, 0:512], xlt[p][:, j * 128:(j + 1) * 128],
                wsb[:, j * HM + half * 512: j * HM + (half + 1) * 512],
                start=(j == 0), stop=False)) for j2 in range(16)] +
                [lambda half=half, pp=pp: nc.tensor.matmul(
                    pp[:, 0:512], on1[0:1, :], bs2b[:, half * 512:(half + 1) * 512],
                    start=False, stop=True)],
                [b_xlt[p], b_wsb, b_consts], [bp])
            pr.emit(V, lambda p=p, half=half, pp=pp: nc.vector.tensor_scalar(
                out=scl[p][:, half * 512:(half + 1) * 512], in0=pp[:, 0:512],
                scalar1=32.0, scalar2=-32.0, op0=OP.min, op1=OP.max), [bp, b_scl[p]], [b_scl[p]])
        pr.emit(S, lambda p=p: nc.scalar.activation(out=esb[p], in_=scl[p], func=AF.Exp),
                [b_scl[p], b_es[p]], [b_es[p]])
        dma(SY, esA[c * 128:(c + 1) * 128, :], esb[p], [b_es[p]], [b_esA])
        dma(G, carryT[c:c + 1, :], carry[:], [b_carry], [b_carryT])
        pr.emit(Tn, [(lambda p=p, half=h2: nc.tensor.matmul(
            pC[0:1, half * 512:(half + 1) * 512], onc[:, 0:1],
            esb[p][:, half * 512:(half + 1) * 512], start=True, stop=True)) for h2 in range(2)],
            [b_es[p], b_consts], [b_pC])
        pr.emit(V, lambda: nc.vector.tensor_tensor(out=carry[:], in0=carry[:], in1=pC[0:1, 0:1024], op=OP.add),
                [b_pC, b_carry], [b_carry])
    pr.barrier()

    wbig = AW[:, 0:16 * 2048]
    wld = [AFr[:, 0:2048], AFr[:, 2048:4096]]
    b_wld = [B_(), B_()]

    def load_weight(wsrc, wdst, bw):
        for j in range(16):
            p = j % 2
            dma(G, wld[p], wsrc[j * 128:(j + 1) * 128, :], [b_wld[p]], [b_wld[p]])
            cast(S if p == 0 else V, wdst[:, j * 2048:(j + 1) * 2048], wld[p], [b_wld[p], bw], [bw])

    krb = [AB[:, 0:2048], AB[:, 2048:4096]]
    vbb = [AB[:, 4096:6144], AB[:, 6144:8192]]
    xltb = [AB[:, 8192:10240], AB[:, 10240:12288]]
    esbb = [AB[:, 12288:13312], AB[:, 13312:14336]]
    b_kr = [B_(), B_()]
    b_vb = [B_(), B_()]
    b_xl = [B_(), B_()]
    b_eb = [B_(), B_()]

    def rope(psrc, b_ps, hoff, ctab, stab, coff, b_ct, dst, b_dst, dsl):
        rset = (dsl // 128) % 4
        rt = rt_sets[rset]
        b_rt = b_rt_sets[rset]
        c0 = ctab[:, coff:coff + 64]
        s0 = stab[:, coff:coff + 64]
        q1 = psrc[:, hoff:hoff + 64]
        q2 = psrc[:, hoff + 64:hoff + 128]
        pr.emit(V, lambda: nc.vector.tensor_tensor(out=rt[0][:], in0=q1, in1=c0, op=OP.mult), [b_ps, b_ct, b_rt], [b_rt])
        pr.emit(V, lambda: nc.vector.tensor_tensor(out=rt[1][:], in0=q2, in1=s0, op=OP.mult), [b_ps, b_ct, b_rt], [b_rt])
        pr.emit(V, lambda: nc.vector.tensor_tensor(out=rt[2][:], in0=q2, in1=c0, op=OP.mult), [b_ps, b_ct, b_rt], [b_rt])
        pr.emit(V, lambda: nc.vector.tensor_tensor(out=rt[3][:], in0=q1, in1=s0, op=OP.mult), [b_ps, b_ct, b_rt], [b_rt])
        pr.emit(V, lambda: nc.vector.tensor_tensor(out=dst[:, dsl:dsl + 64], in0=rt[0][:], in1=rt[1][:], op=OP.subtract), [b_rt, b_dst], [b_dst])
        pr.emit(V, lambda: nc.vector.tensor_tensor(out=dst[:, dsl + 64:dsl + 128], in0=rt[2][:], in1=rt[3][:], op=OP.add), [b_rt, b_dst], [b_dst])

    for PASS in ("K", "V"):
        bw = B_()
        load_weight(wkT if PASS == "K" else wvT, wbig, bw)
        pr.barrier()
        if PASS == "K":
            pr.emit(V, lambda: nc.vector.memset(s1f[:], 0.0), [b_states], [b_states])
        else:
            pr.emit(V, lambda: nc.vector.memset(s2f[:], 0.0), [b_states], [b_states])
        for c in range(NCH):
            p = c % 2
            if c % OCH == 0:
                k = c // OCH
                if PASS == "K":
                    cast(S, s1b[:], s1f[:], [b_states], [b_states])
                    dma(G, snap1[k, :, :], s1b[:], [b_states], [b_sn1])
                else:
                    cast(S, s2b[:], s2f[:], [b_states], [b_states])
                    dma(G, snap2[k, :, :], s2b[0:64, :], [b_states], [b_sn2])
            dma(SY, xltb[p], xltA[:, :, c * 128:(c + 1) * 128].rearrange("a p b -> p a b"),
                [b_xl[p], b_xltA], [b_xl[p]])
            dma(SY, esbb[p], esA[c * 128:(c + 1) * 128, :], [b_eb[p], b_esA], [b_eb[p]])
            if PASS == "K":
                dma(SY, ckb[p][:], cosK[:, c * 64:(c + 1) * 64], [b_ck], [b_ck])
                dma(SY, skb[p][:], sinK[:, c * 64:(c + 1) * 64], [b_ck], [b_ck])
            dst, b_dst = (krb[p], b_kr[p]) if PASS == "K" else (vbb[p], b_vb[p])
            for os_ in range(4):
                pp, bp = (pA, b_pA) if os_ % 2 == 0 else (pB, b_pB)
                pr.emit(Tn, [(lambda p=p, j=j2, os_=os_, pp=pp: nc.tensor.matmul(
                    pp[:, 0:512], xltb[p][:, j * 128:(j + 1) * 128],
                    wbig[:, j * 2048 + os_ * 512: j * 2048 + os_ * 512 + 512],
                    start=(j == 0), stop=(j == 15))) for j2 in range(16)],
                    [b_xl[p], bw], [bp])
                if PASS == "K":
                    for hh in range(2):
                        rope(pp[:, 0:512], bp, hh * 128, ckb[p], skb[p], 0, b_ck,
                             dst, b_dst, os_ * 512 + hh * 128)
                else:
                    pr.emit(V, lambda p=p, os_=os_, pp=pp, dst=dst: nc.vector.tensor_copy(
                        out=dst[:, os_ * 512:(os_ + 1) * 512], in_=pp[:, 0:512]), [bp, b_dst], [b_dst])
            dma(SY, (krA if PASS == "K" else vA)[c * 128:(c + 1) * 128, :], dst, [b_dst],
                [b_krA if PASS == "K" else b_vA])
            for h in range(H):
                ph, bh = (pH1, b_pH1) if h % 2 == 0 else (pH2, b_pH2)
                if PASS == "K":
                    pr.emit(Tn, lambda p=p, h=h, ph=ph: nc.tensor.matmul(
                        ph[:, 0:64], krb[p][:, h * 128:(h + 1) * 128], esbb[p][:, h * 64:(h + 1) * 64],
                        start=True, stop=True), [b_kr[p], b_eb[p]], [bh])
                    pr.emit(V, lambda h=h, ph=ph: nc.vector.tensor_tensor(
                        out=s1f[:, h * 64:(h + 1) * 64], in0=s1f[:, h * 64:(h + 1) * 64],
                        in1=ph[:, 0:64], op=OP.add), [bh, b_states], [b_states])
                else:
                    pr.emit(Tn, lambda p=p, h=h, ph=ph: nc.tensor.matmul(
                        ph[0:64, 128:256], esbb[p][:, h * 64:(h + 1) * 64], vbb[p][:, h * 128:(h + 1) * 128],
                        start=True, stop=True), [b_eb[p], b_vb[p]], [bh])
                    pr.emit(V, lambda h=h, ph=ph: nc.vector.tensor_tensor(
                        out=s2f[0:64, h * 128:(h + 1) * 128], in0=s2f[0:64, h * 128:(h + 1) * 128],
                        in1=ph[0:64, 128:256], op=OP.add), [bh, b_states], [b_states])
        pr.barrier()

    qrb = [AB[:, 0:2048], AB[:, 2048:4096]]
    b_qr = [B_(), B_()]
    gsl = [AFr[:, 4352:6400], AFr[:, 6400:8448]]
    b_gs = [B_(), B_()]
    for PASS in ("Q", "G"):
        bw = B_()
        load_weight(wqT if PASS == "Q" else wgT, wbig, bw)
        pr.barrier()
        for c in range(OCH):
            p = c % 2
            pr.emit(G, lambda c=c, p=p: nc.gpsimd.indirect_dma_start(
                out=xltb[p][:, 0:2048], out_offset=None, in_=xlnA,
                in_offset=bass.IndirectOffsetOnAxis(ap=itok[:, c:c + 1], axis=0)),
                [b_xlnA, b_consts, b_xl[p]], [b_xl[p]])
            for j in range(16):
                pt, bt = (pT1, b_pT1) if j % 2 == 0 else (pT2, b_pT2)
                pr.emit(Tn, lambda p=p, j=j, pt=pt: nc.tensor.transpose(
                    pt[:, 0:128], xltb[p][:, j * 128:(j + 1) * 128], idn[:]), [b_xl[p], b_consts], [bt])
                pr.emit(V, lambda p=p, j=j, pt=pt: nc.vector.tensor_copy(
                    out=krb[p][:, j * 128:(j + 1) * 128], in_=pt[:, 0:128]), [bt, b_kr[p]], [b_kr[p]])
            for os_ in range(4):
                pp, bp = (pA, b_pA) if os_ % 2 == 0 else (pB, b_pB)
                pr.emit(Tn, [(lambda p=p, j=j2, os_=os_, pp=pp: nc.tensor.matmul(
                    pp[:, 0:512], krb[p][:, j * 128:(j + 1) * 128],
                    wbig[:, j * 2048 + os_ * 512: j * 2048 + os_ * 512 + 512],
                    start=(j == 0), stop=(j == 15))) for j2 in range(16)],
                    [b_kr[p], bw], [bp])
                if PASS == "Q":
                    for hh in range(2):
                        rope(pp[:, 0:512], bp, hh * 128, cosq, sinq, c * 64, b_consts,
                             qrb[p], b_qr[p], os_ * 512 + hh * 128)
                else:
                    pr.emit(S, lambda p=p, os_=os_, pp=pp: nc.scalar.activation(
                        out=gsl[p][:, os_ * 512:(os_ + 1) * 512], in_=pp[:, 0:512], func=AF.Silu),
                        [bp, b_gs[p]], [b_gs[p]])
            if PASS == "Q":
                dma(SY, qrS[c * 128:(c + 1) * 128, :], qrb[p], [b_qr[p]], [b_qrS])
            else:
                dma(SY, gsS[c * 128:(c + 1) * 128, :], gsl[p], [b_gs[p]], [b_gsS])
        pr.barrier()

    pr.emit(G, lambda: nc.gpsimd.indirect_dma_start(
        out=s1b[:], out_offset=None, in_=snap1.rearrange("k p f -> (k p) f"),
        in_offset=bass.IndirectOffsetOnAxis(ap=is1[:, 0:1], axis=0)), [b_sn1, b_consts, b_states], [b_states])
    pr.emit(G, lambda: nc.gpsimd.indirect_dma_start(
        out=s2b[:], out_offset=None, in_=snap2.rearrange("k p f -> (k p) f"),
        in_offset=bass.IndirectOffsetOnAxis(ap=is2[:, 0:1], axis=0)), [b_sn2, b_consts, b_states], [b_states])
    pr.emit(G, lambda: nc.gpsimd.indirect_dma_start(
        out=crg[:], out_offset=None, in_=carryT,
        in_offset=bass.IndirectOffsetOnAxis(ap=icr[:, 0:1], axis=0)), [b_carryT, b_consts, b_carry], [b_carry])
    pr.emit(V, lambda: nc.vector.tensor_copy(out=carry[:], in_=crg[0:1, :]), [b_carry], [b_carry])
    cast(V, s1f[:], s1b[:], [b_states], [b_states])
    cast(V, s2f[:], s2b[:], [b_states], [b_states])
    cast(S, carryb[:], carry[:], [b_carry], [b_carry])

    esf = AB[:, 8192:10240]
    b_ef = B_()
    PHA, PHB = pH1, pH2
    b_PHA, b_PHB = b_pH1, b_pH2
    PTA, PTB = pT1, pT2
    b_PTA, b_PTB = b_pT1, b_pT2
    hsm2p = [hsm, [sb(f"hsmB{i}", [128, 128], BF16) for i in range(6)]]
    b_hsm2p = [b_hsm, B_()]
    sm2p = [sm, sb("smB", [128, 8], F32)]
    b_sm2p = [b_sm, B_()]
    small2p = [(qvs, qve, qvp, sqb, onb, ogb),
               (sb("qvsB", [128, 64], F32), sb("qveB", [128, 64], F32), sb("qvpB", [128, 64], BF16),
                sb("sqbB", [128, 128], F32), sb("onbB", [128, 128], F32), sb("ogbB", [128, 128], BF16))]
    esg = [AB[:, 10240:11264], AB[:, 11264:12288]]
    b_eg = [B_(), B_()]
    qr2 = [AB[:, 12288:14336], AB[:, 14336:16384]]
    b_q2 = [B_(), B_()]
    enz = [AFr[:, 0:1024], AFr[:, 1024:2048]]
    b_enz = [B_(), B_()]
    for c in range(OCH):
        p = c % 2
        dma(SY, qr2[p], qrS[c * 128:(c + 1) * 128, :], [b_q2[p], b_qrS], [b_q2[p]])
        dma(SY, gsl[p], gsS[c * 128:(c + 1) * 128, :], [b_gs[p], b_gsS], [b_gs[p]])
        for (dstb, bb, srcA, bA) in ((krb[p], b_kr[p], krA, b_krA), (vbb[p], b_vb[p], vA, b_vA)):
            pr.emit(G, lambda c=c, dstb=dstb, srcA=srcA: nc.gpsimd.indirect_dma_start(
                out=dstb, out_offset=None, in_=srcA,
                in_offset=bass.IndirectOffsetOnAxis(ap=itok[:, c:c + 1], axis=0)),
                [bA, b_consts, bb], [bb])
        pr.emit(G, lambda c=c, p=p: nc.gpsimd.indirect_dma_start(
            out=esg[p], out_offset=None, in_=esA,
            in_offset=bass.IndirectOffsetOnAxis(ap=itok[:, c:c + 1], axis=0)),
            [b_esA, b_consts, b_eg[p]], [b_eg[p]])
        for sfi in range(8):
            pt, bt = (pT1, b_pT1) if sfi % 2 == 0 else (pT2, b_pT2)
            pr.emit(Tn, lambda p=p, sfi=sfi, pt=pt: nc.tensor.transpose(
                pt[:, 0:128], esg[p][:, sfi * 128:(sfi + 1) * 128], idn[:]), [b_eg[p], b_consts], [bt])
            pr.emit(V, lambda sfi=sfi, pt=pt: nc.vector.tensor_copy(
                out=esf[0:64, (2 * sfi) * 128:(2 * sfi + 1) * 128], in_=pt[0:64, 0:128]), [bt, b_ef], [b_ef])
            pr.emit(V, lambda sfi=sfi, pt=pt: nc.vector.tensor_copy(
                out=esf[0:64, (2 * sfi + 1) * 128:(2 * sfi + 2) * 128], in_=pt[64:128, 0:128]), [bt, b_ef], [b_ef])
        for half in range(2):
            pr.emit(Tn, [(lambda p=p, half=half: nc.tensor.matmul(
                pC[:, half * 512:(half + 1) * 512], msk[:],
                esg[p][:, half * 512:(half + 1) * 512], start=True, stop=False)),
                (lambda half=half: nc.tensor.matmul(
                    pC[:, half * 512:(half + 1) * 512], on1[0:1, :],
                    carryb[:, half * 512:(half + 1) * 512], start=False, stop=True))],
                [b_eg[p], b_consts, b_carry], [b_pC])
        pr.emit(V, lambda p=p: nc.vector.reciprocal(enz[p][:], pC[:, 0:1024]), [b_pC, b_enz[p]], [b_enz[p]])
        pr.emit(Tn, [(lambda p=p, half=h2: nc.tensor.matmul(
            pC[0:1, half * 512:(half + 1) * 512], onc[:, 0:1],
            esg[p][:, half * 512:(half + 1) * 512], start=True, stop=True)) for h2 in range(2)],
            [b_eg[p], b_consts, b_enz[p]], [b_pC])
        pr.emit(V, lambda: nc.vector.tensor_tensor(out=carry[:], in0=carry[:], in1=pC[0:1, 0:1024], op=OP.add),
                [b_pC, b_carry], [b_carry])
        cast(S, carryb[:], carry[:], [b_carry], [b_carry])
        for h in range(H):
            hq = h % 2
            pH1, pH2 = (PHA, PHB) if hq == 0 else (PHB, PHA)
            b_pH1, b_pH2 = (b_PHA, b_PHB) if hq == 0 else (b_PHB, b_PHA)
            pT1, pT2 = (PTA, PTB) if hq == 0 else (PTB, PTA)
            b_pT1, b_pT2 = (b_PTA, b_PTB) if hq == 0 else (b_PTB, b_PTA)
            hsm = hsm2p[hq]
            b_hsm = b_hsm2p[hq]
            sm = sm2p[hq]
            b_sm = b_sm2p[hq]
            qvs, qve, qvp, sqb, onb, ogb = small2p[hq]
            pr.emit(Tn, lambda pH1=pH1, pH2=pH2, pT1=pT1, pT2=pT2, hsm=hsm, sm=sm, qvs=qvs, qve=qve, qvp=qvp, sqb=sqb, onb=onb, ogb=ogb, p=p, h=h: nc.tensor.transpose(
                pT1[:, 0:128], qr2[p][:, h * 128:(h + 1) * 128], idn[:]), [b_q2[p], b_consts], [b_pT1])
            pr.emit(V, lambda pH1=pH1, pH2=pH2, pT1=pT1, pT2=pT2, hsm=hsm, sm=sm, qvs=qvs, qve=qve, qvp=qvp, sqb=sqb, onb=onb, ogb=ogb: nc.vector.tensor_copy(out=hsm[0][:], in_=pT1[:, 0:128]), [b_pT1, b_hsm], [b_hsm])
            pr.emit(Tn, lambda pH1=pH1, pH2=pH2, pT1=pT1, pT2=pT2, hsm=hsm, sm=sm, qvs=qvs, qve=qve, qvp=qvp, sqb=sqb, onb=onb, ogb=ogb, p=p, h=h: nc.tensor.transpose(
                pT2[:, 0:128], krb[p][:, h * 128:(h + 1) * 128], idn[:]), [b_kr[p], b_consts], [b_pT2])
            pr.emit(V, lambda pH1=pH1, pH2=pH2, pT1=pT1, pT2=pT2, hsm=hsm, sm=sm, qvs=qvs, qve=qve, qvp=qvp, sqb=sqb, onb=onb, ogb=ogb: nc.vector.tensor_copy(out=hsm[1][:], in_=pT2[:, 0:128]), [b_pT2, b_hsm], [b_hsm])
            pr.emit(Tn, lambda pH1=pH1, pH2=pH2, pT1=pT1, pT2=pT2, hsm=hsm, sm=sm, qvs=qvs, qve=qve, qvp=qvp, sqb=sqb, onb=onb, ogb=ogb: nc.tensor.matmul(pH1[:, 0:128], hsm[1][:], hsm[0][:], start=True, stop=True),
                    [b_hsm], [b_pH1])
            pr.emit(V, lambda pH1=pH1, pH2=pH2, pT1=pT1, pT2=pT2, hsm=hsm, sm=sm, qvs=qvs, qve=qve, qvp=qvp, sqb=sqb, onb=onb, ogb=ogb: nc.vector.tensor_tensor(out=hsm[2][:], in0=pH1[:, 0:128], in1=msk[:], op=OP.mult),
                    [b_pH1, b_consts, b_hsm], [b_hsm])
            pr.emit(Tn, [(lambda pH1=pH1, pH2=pH2, pT1=pT1, pT2=pT2, hsm=hsm, sm=sm, qvs=qvs, qve=qve, qvp=qvp, sqb=sqb, onb=onb, ogb=ogb, p=p, h=h: nc.tensor.matmul(
                pH2[:, 0:64], hsm[2][:], esg[p][:, h * 64:(h + 1) * 64], start=True, stop=False)),
                (lambda pH1=pH1, pH2=pH2, pT1=pT1, pT2=pT2, hsm=hsm, sm=sm, qvs=qvs, qve=qve, qvp=qvp, sqb=sqb, onb=onb, ogb=ogb, h=h: nc.tensor.matmul(
                    pH2[:, 0:64], hsm[0][:], s1b[:, h * 64:(h + 1) * 64], start=False, stop=True))],
                [b_hsm, b_eg[p], b_states], [b_pH2])
            pr.emit(V, lambda pH1=pH1, pH2=pH2, pT1=pT1, pT2=pT2, hsm=hsm, sm=sm, qvs=qvs, qve=qve, qvp=qvp, sqb=sqb, onb=onb, ogb=ogb, p=p, h=h: nc.vector.tensor_tensor(
                out=qvs[:], in0=pH2[:, 0:64], in1=enz[p][:, h * 64:(h + 1) * 64], op=OP.mult),
                [b_pH2, b_enz[p], b_sm], [b_sm])
            pr.emit(V, lambda pH1=pH1, pH2=pH2, pT1=pT1, pT2=pT2, hsm=hsm, sm=sm, qvs=qvs, qve=qve, qvp=qvp, sqb=sqb, onb=onb, ogb=ogb: nc.vector.reduce_max(sm[:, 0:1], qvs[:], axis=AX, negate=True), [b_sm], [b_sm])
            pr.emit(S, lambda pH1=pH1, pH2=pH2, pT1=pT1, pT2=pT2, hsm=hsm, sm=sm, qvs=qvs, qve=qve, qvp=qvp, sqb=sqb, onb=onb, ogb=ogb: nc.scalar.activation(out=qve[:], in_=qvs[:], func=AF.Exp, bias=sm[:, 0:1]),
                    [b_sm], [b_sm])
            pr.emit(V, lambda pH1=pH1, pH2=pH2, pT1=pT1, pT2=pT2, hsm=hsm, sm=sm, qvs=qvs, qve=qve, qvp=qvp, sqb=sqb, onb=onb, ogb=ogb: nc.vector.reduce_sum(sm[:, 1:2], qve[:], axis=AX), [b_sm], [b_sm])
            pr.emit(V, lambda pH1=pH1, pH2=pH2, pT1=pT1, pT2=pT2, hsm=hsm, sm=sm, qvs=qvs, qve=qve, qvp=qvp, sqb=sqb, onb=onb, ogb=ogb: nc.vector.reciprocal(sm[:, 2:3], sm[:, 1:2]), [b_sm], [b_sm])
            ts1(qve[:], qve[:], sm[:, 2:3], OP.mult, [b_sm], [b_sm])
            pr.emit(V, lambda pH1=pH1, pH2=pH2, pT1=pT1, pT2=pT2, hsm=hsm, sm=sm, qvs=qvs, qve=qve, qvp=qvp, sqb=sqb, onb=onb, ogb=ogb, p=p, h=h: nc.vector.tensor_tensor(
                out=qvp[:], in0=qve[:], in1=enz[p][:, h * 64:(h + 1) * 64], op=OP.mult),
                [b_sm, b_enz[p]], [b_sm])
            pr.emit(Tn, lambda pH1=pH1, pH2=pH2, pT1=pT1, pT2=pT2, hsm=hsm, sm=sm, qvs=qvs, qve=qve, qvp=qvp, sqb=sqb, onb=onb, ogb=ogb: nc.tensor.transpose(pT1[0:64, 0:128], qvp[:], idn[:]), [b_sm, b_consts], [b_pT1])
            pr.emit(V, lambda pH1=pH1, pH2=pH2, pT1=pT1, pT2=pT2, hsm=hsm, sm=sm, qvs=qvs, qve=qve, qvp=qvp, sqb=sqb, onb=onb, ogb=ogb: nc.vector.tensor_copy(out=hsm[4][0:64, :], in_=pT1[0:64, 0:128]), [b_pT1, b_hsm], [b_hsm])
            pr.emit(Tn, lambda pH1=pH1, pH2=pH2, pT1=pT1, pT2=pT2, hsm=hsm, sm=sm, qvs=qvs, qve=qve, qvp=qvp, sqb=sqb, onb=onb, ogb=ogb, h=h: nc.tensor.matmul(
                pH1[:, 128:256], esf[0:64, h * 128:(h + 1) * 128],
                hsm[4][0:64, :], start=True, stop=True), [b_ef, b_hsm], [b_pH1])
            pr.emit(V, lambda pH1=pH1, pH2=pH2, pT1=pT1, pT2=pT2, hsm=hsm, sm=sm, qvs=qvs, qve=qve, qvp=qvp, sqb=sqb, onb=onb, ogb=ogb: nc.vector.tensor_tensor(out=hsm[3][:], in0=pH1[:, 128:256], in1=msk[:], op=OP.mult),
                    [b_pH1, b_consts, b_hsm], [b_hsm])
            pr.emit(Tn, [(lambda pH1=pH1, pH2=pH2, pT1=pT1, pT2=pT2, hsm=hsm, sm=sm, qvs=qvs, qve=qve, qvp=qvp, sqb=sqb, onb=onb, ogb=ogb, p=p, h=h: nc.tensor.matmul(
                pH2[:, 64:192], hsm[3][:], vbb[p][:, h * 128:(h + 1) * 128], start=True, stop=False)),
                (lambda pH1=pH1, pH2=pH2, pT1=pT1, pT2=pT2, hsm=hsm, sm=sm, qvs=qvs, qve=qve, qvp=qvp, sqb=sqb, onb=onb, ogb=ogb, h=h: nc.tensor.matmul(
                    pH2[:, 64:192], hsm[4][0:64, :], s2b[0:64, h * 128:(h + 1) * 128], start=False, stop=True))],
                [b_hsm, b_vb[p], b_states], [b_pH2])
            pr.emit(S, lambda pH1=pH1, pH2=pH2, pT1=pT1, pT2=pT2, hsm=hsm, sm=sm, qvs=qvs, qve=qve, qvp=qvp, sqb=sqb, onb=onb, ogb=ogb: nc.scalar.activation(out=sqb[:], in_=pH2[:, 64:192], func=AF.Square),
                    [b_pH2, b_sm], [b_sm])
            pr.emit(V, lambda pH1=pH1, pH2=pH2, pT1=pT1, pT2=pT2, hsm=hsm, sm=sm, qvs=qvs, qve=qve, qvp=qvp, sqb=sqb, onb=onb, ogb=ogb: nc.vector.reduce_sum(sm[:, 3:4], sqb[:], axis=AX), [b_sm], [b_sm])
            pr.emit(S, lambda pH1=pH1, pH2=pH2, pT1=pT1, pT2=pT2, hsm=hsm, sm=sm, qvs=qvs, qve=qve, qvp=qvp, sqb=sqb, onb=onb, ogb=ogb: nc.scalar.activation(out=sm[:, 4:5], in_=sm[:, 3:4], func=AF.Sqrt,
                    bias=epsb[:, 1:2], scale=1.0 / DV), [b_sm, b_consts], [b_sm])
            pr.emit(V, lambda pH1=pH1, pH2=pH2, pT1=pT1, pT2=pT2, hsm=hsm, sm=sm, qvs=qvs, qve=qve, qvp=qvp, sqb=sqb, onb=onb, ogb=ogb: nc.vector.reciprocal(sm[:, 5:6], sm[:, 4:5]), [b_sm], [b_sm])
            ts1(onb[:], pH2[:, 64:192], sm[:, 5:6], OP.mult, [b_pH2, b_sm], [b_sm])
            pr.emit(V, lambda pH1=pH1, pH2=pH2, pT1=pT1, pT2=pT2, hsm=hsm, sm=sm, qvs=qvs, qve=qve, qvp=qvp, sqb=sqb, onb=onb, ogb=ogb: nc.vector.tensor_tensor(out=onb[:], in0=onb[:], in1=gnwb[:], op=OP.mult),
                    [b_consts, b_sm], [b_sm])
            pr.emit(V, lambda pH1=pH1, pH2=pH2, pT1=pT1, pT2=pT2, hsm=hsm, sm=sm, qvs=qvs, qve=qve, qvp=qvp, sqb=sqb, onb=onb, ogb=ogb, p=p, h=h: nc.vector.tensor_tensor(
                out=ogb[:], in0=onb[:], in1=gsl[p][:, h * 128:(h + 1) * 128], op=OP.mult),
                [b_gs[p], b_sm], [b_sm])
            pr.emit(Tn, lambda pH1=pH1, pH2=pH2, pT1=pT1, pT2=pT2, hsm=hsm, sm=sm, qvs=qvs, qve=qve, qvp=qvp, sqb=sqb, onb=onb, ogb=ogb: nc.tensor.transpose(pT2[:, 0:128], ogb[:], idn[:]), [b_sm, b_consts], [b_pT2])
            pr.emit(V, lambda pH1=pH1, pH2=pH2, pT1=pT1, pT2=pT2, hsm=hsm, sm=sm, qvs=qvs, qve=qve, qvp=qvp, sqb=sqb, onb=onb, ogb=ogb: nc.vector.tensor_copy(out=hsm[5][:], in_=pT2[:, 0:128]), [b_pT2, b_hsm], [b_hsm])
            dma(SY, ogS[h, :, c * 128:(c + 1) * 128], hsm[5][:], [b_hsm], [b_ogS])
            pr.emit(Tn, lambda pH1=pH1, pH2=pH2, pT1=pT1, pT2=pT2, hsm=hsm, sm=sm, qvs=qvs, qve=qve, qvp=qvp, sqb=sqb, onb=onb, ogb=ogb, p=p, h=h: nc.tensor.matmul(
                pH1[:, 256:320], krb[p][:, h * 128:(h + 1) * 128], esg[p][:, h * 64:(h + 1) * 64],
                start=True, stop=True), [b_kr[p], b_eg[p]], [b_pH1])
            pr.emit(V, lambda pH1=pH1, pH2=pH2, pT1=pT1, pT2=pT2, hsm=hsm, sm=sm, qvs=qvs, qve=qve, qvp=qvp, sqb=sqb, onb=onb, ogb=ogb, h=h: nc.vector.tensor_tensor(
                out=s1f[:, h * 64:(h + 1) * 64], in0=s1f[:, h * 64:(h + 1) * 64],
                in1=pH1[:, 256:320], op=OP.add), [b_pH1, b_states], [b_states])
            cast(S, s1b[:, h * 64:(h + 1) * 64], s1f[:, h * 64:(h + 1) * 64], [b_states], [b_states])
            pr.emit(Tn, lambda pH1=pH1, pH2=pH2, pT1=pT1, pT2=pT2, hsm=hsm, sm=sm, qvs=qvs, qve=qve, qvp=qvp, sqb=sqb, onb=onb, ogb=ogb, p=p, h=h: nc.tensor.matmul(
                pH1[0:64, 320:448], esg[p][:, h * 64:(h + 1) * 64], vbb[p][:, h * 128:(h + 1) * 128],
                start=True, stop=True), [b_eg[p], b_vb[p]], [b_pH1])
            pr.emit(V, lambda pH1=pH1, pH2=pH2, pT1=pT1, pT2=pT2, hsm=hsm, sm=sm, qvs=qvs, qve=qve, qvp=qvp, sqb=sqb, onb=onb, ogb=ogb, h=h: nc.vector.tensor_tensor(
                out=s2f[0:64, h * 128:(h + 1) * 128], in0=s2f[0:64, h * 128:(h + 1) * 128],
                in1=pH1[0:64, 320:448], op=OP.add), [b_pH1, b_states], [b_states])
            cast(S, s2b[0:64, h * 128:(h + 1) * 128], s2f[0:64, h * 128:(h + 1) * 128], [b_states], [b_states])
    pr.barrier()

    b_wob = B_()
    load_weight(woT, wbig, b_wob)
    pr.barrier()
    ogl = [AB[:, 0:2048], AB[:, 2048:4096]]
    b_ogl = [B_(), B_()]
    xob = [AFr[:, 0:2048], AFr[:, 2048:4096]]
    b_xo = [B_(), B_()]
    x2b = [AFr[:, 4096:6144], AFr[:, 6144:8192]]
    b_x2 = [B_(), B_()]
    xsq3 = AFr[:, 8192:10240]
    b_xsq3 = B_()
    xl2 = [AB[:, 4096:6144], AB[:, 6144:8192]]
    b_xl2 = [B_(), B_()]
    xlt2 = [AB[:, 8192:10240], AB[:, 10240:12288]]
    b_xlt2 = [B_(), B_()]
    for c in range(OCH):
        p = c % 2
        dma(SY, ogl[p], ogS[:, :, c * 128:(c + 1) * 128].rearrange("a p b -> p a b"),
            [b_ogS, b_ogl[p]], [b_ogl[p]])
        dma(SY, xob[p], x_own[c * 128:(c + 1) * 128, :], [b_xo[p]], [b_xo[p]])
        for os_ in range(4):
            pp, bp = (pA, b_pA) if os_ % 2 == 0 else (pB, b_pB)
            pr.emit(Tn, [(lambda p=p, j=j2, os_=os_, pp=pp: nc.tensor.matmul(
                pp[:, 0:512], ogl[p][:, j * 128:(j + 1) * 128],
                wbig[:, j * 2048 + os_ * 512: j * 2048 + os_ * 512 + 512],
                start=(j == 0), stop=(j == 15))) for j2 in range(16)],
                [b_ogl[p], b_wob], [bp])
            pr.emit(V, lambda p=p, os_=os_, pp=pp: nc.vector.tensor_tensor(
                out=x2b[p][:, os_ * 512:(os_ + 1) * 512], in0=xob[p][:, os_ * 512:(os_ + 1) * 512],
                in1=pp[:, 0:512], op=OP.add), [b_xo[p], bp, b_x2[p]], [b_x2[p]])
        dma(SY, x2S[c * 128:(c + 1) * 128, :], x2b[p], [b_x2[p]], [b_x2S])
        ln_pipe(x2b[p], b_x2[p], xsq3, b_xsq3, xl2[p], b_xl2[p], lnst2p[p], b_ln2p[p])
        for j in range(16):
            pt, bt = (pT1, b_pT1) if j % 2 == 0 else (pT2, b_pT2)
            pr.emit(Tn, lambda p=p, j=j, pt=pt: nc.tensor.transpose(
                pt[:, 0:128], xl2[p][:, j * 128:(j + 1) * 128], idn[:]), [b_xl2[p], b_consts], [bt])
            pr.emit(V, lambda p=p, j=j, pt=pt: nc.vector.tensor_copy(
                out=xlt2[p][:, j * 128:(j + 1) * 128], in_=pt[:, 0:128]), [bt, b_xlt2[p]], [b_xlt2[p]])
        dma(SY, xl2T[:, :, c * 128:(c + 1) * 128].rearrange("a p b -> p a b"), xlt2[p], [b_xlt2[p]], [b_xl2T])
        pr.emit(Tn, [(lambda p=p, j=j2: nc.tensor.matmul(
            pC[:, 0:8], xlt2[p][:, j * 128:(j + 1) * 128], gwb[:, j * 8:(j + 1) * 8],
            start=(j == 0), stop=(j == 15))) for j2 in range(16)],
            [b_xlt2[p], b_consts], [b_pC])
        pr.emit(V, lambda: nc.vector.tensor_copy(out=lg[:], in_=pC[:, 0:8]), [b_pC, b_router], [b_router])
        pr.emit(V, lambda: nc.vector.reduce_max(sm[:, 0:1], lg[:], axis=AX), [b_router, b_sm], [b_sm])
        ts1(lgm[:], lg[:], sm[:, 0:1], OP.is_ge, [b_router, b_sm], [b_router])
        ts1(lg2[:], lgm[:], -1e9, OP.mult, [b_router], [b_router])
        pr.emit(V, lambda: nc.vector.tensor_tensor(out=lg2[:], in0=lg[:], in1=lg2[:], op=OP.add),
                [b_router], [b_router])
        pr.emit(V, lambda: nc.vector.reduce_max(sm[:, 1:2], lg2[:], axis=AX), [b_router, b_sm], [b_sm])
        ts1(lgm2[:], lg2[:], sm[:, 1:2], OP.is_ge, [b_router, b_sm], [b_router])
        pr.emit(V, lambda: nc.vector.tensor_tensor(out=sm[:, 2:3], in0=sm[:, 1:2], in1=sm[:, 0:1],
                op=OP.subtract), [b_sm], [b_sm])
        pr.emit(S, lambda: nc.scalar.activation(out=sm[:, 3:4], in_=sm[:, 2:3], func=AF.Exp), [b_sm], [b_sm])
        ts1(sm[:, 4:5], sm[:, 3:4], 1.0, OP.add, [b_sm], [b_sm])
        pr.emit(V, lambda: nc.vector.reciprocal(sm[:, 5:6], sm[:, 4:5]), [b_sm], [b_sm])
        pr.emit(V, lambda: nc.vector.tensor_tensor(out=sm[:, 6:7], in0=sm[:, 3:4], in1=sm[:, 5:6],
                op=OP.mult), [b_sm], [b_sm])
        ts1(lgm[:], lgm[:], sm[:, 5:6], OP.mult, [b_router, b_sm], [b_router])
        ts1(lgm2[:], lgm2[:], sm[:, 6:7], OP.mult, [b_router, b_sm], [b_router])
        pr.emit(V, lambda: nc.vector.tensor_tensor(out=twf[:], in0=lgm[:], in1=lgm2[:], op=OP.add),
                [b_router], [b_router])
        cast(S, twb[:], twf[:], [b_router], [b_router])
        pr.emit(Tn, lambda: nc.tensor.transpose(pT1[0:8, 0:128], twb[:], idn[:]), [b_router, b_consts], [b_pT1])
        pr.emit(V, lambda c=c: nc.vector.tensor_copy(out=twTb[0:8, c * 128:(c + 1) * 128],
                in_=pT1[0:8, 0:128]), [b_pT1, b_twT], [b_twT])
    pr.barrier()

    FH = ((0, 6), (6, 11))
    gbufh = AW[:, 0:16 * 768]
    ubufh = AW[:, 16 * 768:32 * 768]
    dbufh = AW[:, 32 * 768:32 * 768 + 6 * 2048]
    b_gb, b_ub, b_db = B_(), B_(), B_()
    xw = AB[:, 0:16 * WSZ]
    b_xw = B_()
    hp = AB[:, 16 * WSZ:16 * WSZ + 6 * WSZ]
    b_hp = B_()
    twbc = AFr[:, 0:WSZ]
    b_twbc = B_()
    sil = [AFr[:, 512:1024], AFr[:, 1024:1536]]
    b_sil = [B_(), B_()]
    aci = [AFr[:, 1536:3584], AFr[:, 3584:5632]]
    b_aci = [B_(), B_()]
    stb = [AFr[:, 5632:7680], AFr[:, 7680:9728]]
    b_st = [B_(), B_()]
    x2i = aci

    step = 0
    NSTEP = E * 2
    for e in range(E):
        for fh in range(2):
            f0, f1 = FH[fh]
            nfs = f1 - f0
            for j in range(16):
                dma(G, gbufh[:, j * 768:j * 768 + nfs * 128],
                    gpB[e, j, :, f0 * 128:f1 * 128], [b_gpB, b_gb], [b_gb])
                dma(G, ubufh[:, j * 768:j * 768 + nfs * 128],
                    upB[e, j, :, f0 * 128:f1 * 128], [b_upB, b_ub], [b_ub])
            for j in range(nfs):
                dma(G, dbufh[:, j * 2048:(j + 1) * 2048], dpB[e, f0 + j, :, :],
                    [b_dpB, b_db], [b_db])
            for w in range(NW):
                dma(SY, xw[:], xl2T[:, :, w * WSZ:(w + 1) * WSZ].rearrange("a p b -> p a b"),
                    [b_xl2T, b_xw], [b_xw])
                pr.emit(Tn, lambda e=e, w=w: nc.tensor.matmul(
                    pC[:, 0:WSZ], selE[:, e * 128:(e + 1) * 128], twTb[0:8, w * WSZ:(w + 1) * WSZ],
                    start=True, stop=True), [b_consts, b_twT, b_twbc], [b_pC])
                pr.emit(V, lambda: nc.vector.tensor_copy(out=twbc[:], in_=pC[:, 0:WSZ]), [b_pC, b_twbc], [b_twbc])
                for fi in range(nfs):
                    fp = fi % 2
                    pg, bg = (pA, b_pA) if fp == 0 else (pB, b_pB)
                    pu, bu = (pH1, b_pH1) if fp == 0 else (pH2, b_pH2)
                    pr.emit(Tn, [(lambda j=j2, fi=fi, pg=pg: nc.tensor.matmul(
                        pg[:, 0:WSZ], gbufh[:, j * 768 + fi * 128: j * 768 + fi * 128 + 128],
                        xw[:, j * WSZ:(j + 1) * WSZ], start=(j == 0), stop=(j == 15))) for j2 in range(16)],
                        [b_gb, b_xw], [bg])
                    pr.emit(Tn, [(lambda j=j2, fi=fi, pu=pu: nc.tensor.matmul(
                        pu[:, 0:WSZ], ubufh[:, j * 768 + fi * 128: j * 768 + fi * 128 + 128],
                        xw[:, j * WSZ:(j + 1) * WSZ], start=(j == 0), stop=(j == 15))) for j2 in range(16)],
                        [b_ub, b_xw], [bu])
                    pr.emit(S, lambda fp=fp, pg=pg: nc.scalar.activation(
                        out=sil[fp][:, 0:WSZ], in_=pg[:, 0:WSZ], func=AF.Silu), [bg, b_sil[fp]], [b_sil[fp]])
                    pr.emit(V, lambda fp=fp, pu=pu: nc.vector.tensor_tensor(
                        out=sil[fp][:, 0:WSZ], in0=sil[fp][:, 0:WSZ], in1=pu[:, 0:WSZ], op=OP.mult),
                        [b_sil[fp], bu], [b_sil[fp]])
                    pr.emit(V, lambda fp=fp, fi=fi: nc.vector.tensor_tensor(
                        out=hp[:, fi * WSZ:(fi + 1) * WSZ], in0=sil[fp][:, 0:WSZ], in1=twbc[:], op=OP.mult),
                        [b_sil[fp], b_twbc, b_hp], [b_hp])
                for s4 in range(NT4):
                    p = s4 % 2
                    row = w * WSZ + s4 * 128
                    for ds in range(4):
                        pd, bd = (pH1, b_pH1) if ds % 2 == 0 else (pH2, b_pH2)
                        pr.emit(Tn, [(lambda j=j2, s4=s4, ds=ds, pd=pd: nc.tensor.matmul(
                            pd[:, 0:512], hp[:, j * WSZ + s4 * 128: j * WSZ + s4 * 128 + 128],
                            dbufh[:, j * 2048 + ds * 512: j * 2048 + ds * 512 + 512],
                            start=(j == 0), stop=(j == nfs - 1))) for j2 in range(nfs)],
                            [b_hp, b_db], [bd])
                        if step == 0:
                            pr.emit(V, lambda p=p, ds=ds, pd=pd: nc.vector.tensor_copy(
                                out=stb[p][:, ds * 512:(ds + 1) * 512], in_=pd[:, 0:512]),
                                [bd, b_st[p]], [b_st[p]])
                        else:
                            if ds == 0:
                                dma(SY, aci[p], accS[row:row + 128, :], [b_accS, b_aci[p]], [b_aci[p]])
                            pr.emit(V, lambda p=p, ds=ds, pd=pd: nc.vector.tensor_tensor(
                                out=stb[p][:, ds * 512:(ds + 1) * 512], in0=aci[p][:, ds * 512:(ds + 1) * 512],
                                in1=pd[:, 0:512], op=OP.add), [bd, b_aci[p], b_st[p]], [b_st[p]])
                    if step == NSTEP - 1:
                        dma(SY, x2i[p], x2S[row:row + 128, :], [b_x2S, b_aci[p]], [b_aci[p]])
                        pr.emit(V, lambda p=p: nc.vector.tensor_tensor(
                            out=stb[p][:], in0=stb[p][:], in1=x2i[p][:], op=OP.add),
                            [b_aci[p], b_st[p]], [b_st[p]])
                        dma(SY, out[row:row + 128, :], stb[p][:], [b_st[p]], [b_out])
                    else:
                        dma(SY, accS[row:row + 128, :], stb[p][:], [b_st[p]], [b_accS, b_st[p]])
            step += 1
    pr.barrier()

    sems = {}
    for e in pr.ops:
        sems[e] = st.enter_context(nc.semaphore(f"sem_{e}"))
    DMAE = {"sy", "g"}

    def replay(eng_name, eng):
        mult = 16 if eng_name in DMAE else 1
        for item in pr.ops[eng_name]:
            if item[0] == "w":
                _, e2, c2 = item
                eng.wait_ge(sems[e2], c2 * (16 if e2 in DMAE else 1))
            else:
                _, fns, c = item
                inst = None
                for f in fns:
                    inst = f()
                inst.then_inc(sems[eng_name], mult)

    with nc.Block() as block:
        @block.tensor
        def _(te):
            replay("t", te)

        @block.vector
        def _(ve):
            replay("v", ve)

        @block.scalar
        def _(se):
            replay("s", se)

        @block.sync
        def _(se):
            replay("sy", se)

        @block.gpsimd
        def _(ge):
            replay("g", ge)

    st.close()
    return nc


_NC_CACHE = {}


def get_nc(T):
    if T not in _NC_CACHE:
        _NC_CACHE[T] = build_nc(T)
    return _NC_CACHE[T]


def kernel(L0_attn_hidden_states, L0_attn_Wq, L0_attn_Wk, L0_attn_Wv, L0_attn_Wg,
           L0_attn_Wo, L0_attn_Ws1, L0_attn_Ws2, L0_attn_bs2, L0_attn_g_norm_weight,
           L0_ffn_gate_w, L0_ffn_gate_proj, L0_ffn_up_proj, L0_ffn_down_proj):
    x = np.asarray(L0_attn_hidden_states, dtype=np.float32)
    Bb, T, _ = x.shape
    OT = T // 4
    OCH = OT // 128
    NCH = T // 128
    nc = get_nc(T)

    f32c = lambda a: np.ascontiguousarray(np.asarray(a, dtype=np.float32))
    wqT = f32c(np.asarray(L0_attn_Wq).T)
    wkT = f32c(np.asarray(L0_attn_Wk).T)
    wvT = f32c(np.asarray(L0_attn_Wv).T)
    wgT = f32c(np.asarray(L0_attn_Wg).T)
    woT = f32c(np.asarray(L0_attn_Wo).T)
    ws1 = f32c(L0_attn_Ws1)
    ws2T = f32c(np.asarray(L0_attn_Ws2).T)
    bs2 = f32c(L0_attn_bs2).reshape(1, HM)
    gnw = f32c(L0_attn_g_norm_weight).reshape(1, DV)
    gwT = f32c(np.asarray(L0_ffn_gate_w).T)
    gpT = f32c(np.asarray(L0_ffn_gate_proj).transpose(0, 2, 1))
    upT = f32c(np.asarray(L0_ffn_up_proj).transpose(0, 2, 1))
    dpT = f32c(np.asarray(L0_ffn_down_proj).transpose(0, 2, 1))

    pos = np.arange(T, dtype=np.float64)
    inv = 1.0 / (10000.0 ** (np.arange(0, DK, 2, dtype=np.float64) / DK))
    ang = pos[:, None] * inv[None, :]
    cosv = np.cos(ang).astype(np.float32)
    sinv = np.sin(ang).astype(np.float32)
    to_tab = lambda a: np.ascontiguousarray(
        a.reshape(-1, 128, 64).transpose(1, 0, 2).reshape(128, -1)).astype(BF)
    cosKt = to_tab(cosv)
    sinKt = to_tab(sinv)
    cosQf = to_tab(cosv * np.float32(SCALE))
    sinQf = to_tab(sinv * np.float32(SCALE))

    tjj = np.arange(128)[:, None]
    tqq = np.arange(128)[None, :]
    maskU = (tjj <= tqq).astype(BF)
    identD = np.eye(128, dtype=BF)
    ones1D = np.ones((1, 128), dtype=BF)
    selED = np.zeros((8, E * 128), dtype=BF)
    for e in range(E):
        selED[e, e * 128:(e + 1) * 128] = 1.0

    in_maps = []
    for c in range(8):
        b, q = c // 4, c % 4
        in_maps.append({
            "x_full": np.ascontiguousarray(x[b]),
            "x_own": np.ascontiguousarray(x[b, q * OT:(q + 1) * OT]),
            "wqT": wqT, "wkT": wkT, "wvT": wvT, "wgT": wgT, "woT": woT,
            "ws1": ws1, "ws2T": ws2T, "bs2": bs2, "gnw": gnw, "gwT": gwT,
            "gpT": gpT, "upT": upT, "dpT": dpT,
            "cosK": cosKt, "sinK": sinKt,
            "cosQ": np.ascontiguousarray(cosQf[:, q * OCH * 64:(q + 1) * OCH * 64]),
            "sinQ": np.ascontiguousarray(sinQf[:, q * OCH * 64:(q + 1) * OCH * 64]),
            "maskU": maskU, "identD": identD, "ones1D": ones1D, "selED": selED,
            "idx_tok": np.ascontiguousarray(
                (q * OT + np.arange(OCH)[None, :] * 128 + np.arange(128)[:, None]).astype(np.int32)),
            "idx_s1": (q * 128 + np.arange(128)[:, None]).astype(np.int32),
            "idx_s2": (q * 64 + np.arange(64)[:, None]).astype(np.int32),
            "idx_cr": np.array([[q * (NCH // 4)], [q * (NCH // 4)]], dtype=np.int32),
        })
    res = run_bass_kernel_spmd(nc, in_maps, core_ids=list(range(8)))
    outp = np.empty((Bb, T, D), dtype=np.float32)
    for c in range(8):
        b, q = c // 4, c % 4
        outp[b, q * OT:(q + 1) * OT] = res.results[c]["out"]
    return outp
